# version 03 (final)
# baseline (speedup 1.0000x reference)
import math
import os
from contextlib import ExitStack


import numpy as np
import ml_dtypes

import concourse.bass as bass
import concourse.mybir as mybir
from concourse.bass_utils import run_bass_kernel_spmd

F32 = mybir.dt.float32
BF16 = mybir.dt.bfloat16
AX = mybir.AxisListType.X
ALU = mybir.AluOpType
ACT = mybir.ActivationFunctionType

NCORE = 8
S = 8192
D = 2048
TPC = 1024
NH = 16
FFH = 5632
NHT = FFH // 128
ER = 512
LN_EPS = 1e-5
RMS_EPS = 1e-6
FFN_EPS = 1e-5
MAGIC = float(1.5 * 2.0 ** 23)
NEGBIG = -32768.0
L1_SCALE = 1.0 / math.sqrt(128.0)
L1_CST = 11.4
ALIBI_SLOPES = [1.0 / 2.0 ** (i / 2.0) for i in range(NH)]

_PARTS = ("L0A", "L0B", "L0C", "L1A", "L1B", "L1C")
_DBG = None
_PROJ = "qkv"
_SLABQ = "sync"
_NSETTLE = 12
_KVLATE = False
_LAST_RES = None


def _kcut(h, qi):
    d = int(math.ceil(261.0 / ALIBI_SLOPES[h]))
    a = 1024 * qi - 511 - d
    if a <= 0:
        return 0
    return min((a + 511) // 512, 2 * qi)



def _tile_lhsT(w_in_out, fcols=128):
    n_in, n_out = w_in_out.shape
    nf, nch = n_out // fcols, n_in // 128
    a = np.asarray(w_in_out, dtype=np.float32).reshape(nch, 128, nf, fcols)
    return np.ascontiguousarray(a.transpose(2, 0, 1, 3))


def _weight_layout():
    ent, off = {}, 0
    def add(name, nf, nch, fcols=128):
        nonlocal off
        ent[name] = (off, nf, nch, fcols)
        off += nf * nch * 128 * fcols
    add("qkv", 48, 16)
    add("outw", 16, 16)
    add("w13", 88, 16)
    add("w2", 16, 44)
    add("wq", 16, 16)
    add("wk", 4, 16)
    add("wv", 4, 16)
    add("wo", 16, 16)
    add("moeA", 4, 16)
    add("moeB", 16, 4)
    add("wg", 1, 16, fcols=8)
    off += (-off) % (8 * 128)
    return ent, off


def _pack_weights(inp):
    ent, tot = _weight_layout()
    flat = np.zeros(tot, dtype=np.float32)
    def put(name, tiles):
        off = ent[name][0]
        flat[off:off + tiles.size] = tiles.reshape(-1)
    put("qkv", _tile_lhsT(np.asarray(inp["L0_attn_qkv_w"], dtype=np.float32)))
    put("outw", _tile_lhsT(np.asarray(inp["L0_attn_out_w"], dtype=np.float32)))
    put("w13", _tile_lhsT(np.asarray(inp["L0_ffn_w13"], dtype=np.float32).T))
    put("w2", _tile_lhsT(np.asarray(inp["L0_ffn_w2"], dtype=np.float32).T))
    put("wq", _tile_lhsT(np.asarray(inp["L1_attn_wq"], dtype=np.float32)))
    put("wk", _tile_lhsT(np.asarray(inp["L1_attn_wk"], dtype=np.float32)))
    put("wv", _tile_lhsT(np.asarray(inp["L1_attn_wv"], dtype=np.float32)))
    put("wo", _tile_lhsT(np.asarray(inp["L1_attn_wo"], dtype=np.float32)))
    put("moeA", _tile_lhsT(np.asarray(inp["L1_ffn_A"], dtype=np.float32).reshape(ER, D).T))
    put("moeB", _tile_lhsT(np.asarray(inp["L1_ffn_Bw"], dtype=np.float32)
                           .transpose(0, 2, 1).reshape(ER, D)))
    put("wg", _tile_lhsT(np.asarray(inp["L1_ffn_Wg"], dtype=np.float32).T, fcols=8))
    return ent, flat


def _rope_tables():
    inv = (np.float32(1.0) / (np.float32(100000.0) **
           (np.arange(64, dtype=np.float32) / np.float32(64.0)))).astype(np.float32)
    t = np.arange(S, dtype=np.float32)
    fr = (t[:, None] * inv[None, :]).astype(np.float32)
    c = np.cos(fr).astype(np.float32)
    s = np.sin(fr).astype(np.float32)
    C = np.ascontiguousarray(np.concatenate([c.T, c.T], axis=0))
    S2 = np.ascontiguousarray(np.concatenate([s.T, -s.T], axis=0))
    return C, S2


def _alibi_ramps():
    out = np.zeros((NH, 2, 512), dtype=np.float32)
    col = np.arange(512)
    jperm = (8 * (col % 64) + col // 64).astype(np.float64)
    bf = ml_dtypes.bfloat16
    for h in range(NH):
        r = ALIBI_SLOPES[h] * jperm
        hi = r.astype(np.float32).astype(bf).astype(np.float32)
        lo = (r - hi).astype(np.float32).astype(bf).astype(np.float32)
        out[h, 0], out[h, 1] = hi, lo
    return out


def _masks(core):
    bf = ml_dtypes.bfloat16
    col = np.arange(512)
    kperm = 8 * (col % 64) + col // 64
    p = np.arange(128)
    m0 = (kperm[None, :] > (8 * p + core)[:, None]) * NEGBIG
    m1 = (512 + kperm[None, :] > (8 * p + core)[:, None]) * NEGBIG
    pk = np.arange(128)
    jq = np.arange(256)
    mT = np.zeros((16, 128, 256), dtype=np.float32)
    for a in range(16):
        koff = 512 * (a // 4) + 8 * (pk % 64) + (2 * (a % 4) + pk // 64)
        mT[a] = (koff[:, None] > (8 * jq + core)[None, :]) * NEGBIG
    return (m0.astype(np.float32).astype(bf), m1.astype(np.float32).astype(bf),
            mT.astype(bf))



ENGS = ("sync", "gpsimd", "tensor", "scalar", "vector")
SEMS = ("LD", "GD", "CC", "T", "S", "V", "CK", "CKG")


def _flat(after):
    out = []
    for t in after or ():
        if t is None:
            continue
        if isinstance(t, list):
            out.extend(x for x in t if x is not None)
        else:
            out.append(t)
    return out


class Emit:
    def __init__(self, nc, sems, ename, eng):
        self.nc, self.sems, self.ename, self.eng = nc, sems, ename, eng
        self.cnt = {k: 0 for k in SEMS}
        self.wm = {e: {k: 0 for k in SEMS} for e in ENGS}
        self.base = []

    def wait(self, engine, tok):
        sem, val = tok
        assert val <= self.cnt[sem], f"wait ahead of emission: {sem} {val} > {self.cnt[sem]}"
        if self.wm[engine][sem] >= val:
            return
        self.wm[engine][sem] = val
        if self.ename == engine:
            self.eng.wait_ge(self.sems[sem], val)

    def _pre(self, engine, after):
        for t in self.base:
            self.wait(engine, t)
        for t in _flat(after):
            self.wait(engine, t)

    def _inc(self, sem, n, inst, attach=None):
        self.cnt[sem] += n
        if inst is not None:
            inst.then_inc(self.sems[sem], attach if attach is not None else n)
        return (sem, self.cnt[sem])

    def barrier(self):
        self.ckpt("sync")
        self.ckpt("gpsimd")
        self.base = [(k, self.cnt[k]) for k in SEMS
                     if self.cnt[k] > 0 and k not in ("LD", "GD")]

    def dma(self, q, out, in_, after=(), **kw):
        self._pre(q, after)
        inst = self.eng.dma_start(out=out, in_=in_, **kw) if self.ename == q else None
        return self._inc("LD" if q == "sync" else "GD", 16, inst)

    def ckpt(self, q, after=()):
        self._pre(q, after)
        sem = "LD" if q == "sync" else "GD"
        self.wait(q, (sem, self.cnt[sem]))
        inst = self.eng.nop() if self.ename == q else None
        return self._inc("CK" if q == "sync" else "CKG", 1, inst)

    def allgather(self, in_ap, out_ap, after=()):
        self._pre("gpsimd", after)
        inst = None
        if self.ename == "gpsimd":
            inst = self.eng.collective_compute(
                "AllGather", ALU.bypass, replica_groups=[list(range(NCORE))],
                ins=[in_ap], outs=[out_ap])
        return self._inc("CC", 1, inst)

    def mm(self, out, lhsT, rhs, start, stop, after=(), tag=False):
        self._pre("tensor", after)
        inst = None
        if self.ename == "tensor":
            inst = self.eng.matmul(out, lhsT, rhs, start=start, stop=stop)
        return self._inc("T", 1, inst) if tag else None

    def transpose(self, out, in_, ident, after=(), tag=True):
        self._pre("tensor", after)
        inst = None
        if self.ename == "tensor":
            inst = self.eng.transpose(out, in_, ident)
        return self._inc("T", 1, inst) if tag else None

    def act(self, out, in_, func, bias=0.0, scale=1.0, accum_out=None, after=(), tag=True):
        self._pre("scalar", after)
        inst = None
        if self.ename == "scalar":
            inst = self.eng.activation(out, in_, func, bias=bias, scale=scale,
                                       accum_out=accum_out)
        return self._inc("S", 1, inst) if tag else None

    def scopy(self, out, in_, after=(), tag=True):
        self._pre("scalar", after)
        inst = self.eng.copy(out, in_) if self.ename == "scalar" else None
        return self._inc("S", 1, inst) if tag else None

    def ssqrt(self, out, in_, after=(), tag=True):
        self._pre("scalar", after)
        inst = self.eng.sqrt(out, in_) if self.ename == "scalar" else None
        return self._inc("S", 1, inst) if tag else None

    def _v(self, fn, after, tag):
        self._pre("vector", after)
        inst = fn(self.eng) if self.ename == "vector" else None
        return self._inc("V", 1, inst) if tag else None

    def tt(self, out, a, b, op, after=(), tag=True):
        return self._v(lambda e: e.tensor_tensor(out, a, b, op), after, tag)

    def _vflush(self, out, tag):
        self._v(lambda e: e.memset(self.vpad, 0.0), (), False)
        np_, w = out.shape[0], out.shape[1]
        wit = out[:, w - 1:w] if w > 1 else out
        return self._v(lambda e: e.tensor_copy(self.vpad2[0:np_, :], wit), (), tag)

    def ts(self, out, in0, s1, s2, op0, op1=None, after=(), tag=True):
        if op1 is None:
            return self._v(lambda e: e.tensor_scalar(out, in0, s1, None, op0), after, tag)
        self._v(lambda e: e.tensor_scalar(out, in0, s1, s2, op0, op1), after, False)
        return self._vflush(out, tag)

    def stt(self, out, in0, sc, in1, op0, op1, after=(), tag=True):
        self._v(lambda e: e.scalar_tensor_tensor(out, in0, sc, in1, op0, op1),
                after, False)
        return self._vflush(out, tag)

    def red(self, out, in_, op, absval=False, negate=False, after=(), tag=True):
        self._v(lambda e: e.tensor_reduce(out, in_, AX, op,
                                          apply_absolute_value=absval,
                                          negate=negate), after, False)
        return self._vflush(out, tag)

    def recip(self, out, in_, after=(), tag=True):
        return self._v(lambda e: e.reciprocal(out, in_), after, tag)

    def vcopy(self, out, in_, after=(), tag=True):
        return self._v(lambda e: e.tensor_copy(out, in_), after, tag)

    def vmemset(self, ap, val, after=(), tag=True):
        return self._v(lambda e: e.memset(ap, val), after, tag)


class Arena:
    def __init__(self, ap_u8, nbytes):
        self.ap, self.nb, self.off = ap_u8, nbytes, 0

    def alloc(self, cols, dtype):
        size = ({F32: 4, BF16: 2}[dtype] * cols + 31) & ~31
        assert self.off + size <= self.nb, f"arena overflow {self.off}+{size}>{self.nb}"
        view = self.ap[:, self.off:self.off + size].bitcast(dtype)
        self.off += size
        return view

    def mark(self):
        return self.off

    def reset(self, m=0):
        self.off = m



def bfree(E, key):
    return E["_free"].get(key)


def setfree(E, key, *toks):
    E["_free"][key] = [t for t in toks if t is not None]



def build_nc(went, wtot):
    nc = bass.Bass("TRN2", target_bir_lowering=False, debug=False)
    E = {"went": went, "_free": {}}

    def inp(name, shape, dt):
        return nc.dram_tensor(name, shape, dt, kind="ExternalInput").ap()

    E["xin"] = inp("xin", [TPC, D], F32)
    E["wsh"] = inp("wsh", [wtot // NCORE], F32)
    E["ropeC_in"] = inp("ropeC", [128, TPC], F32)
    E["ropeS_in"] = inp("ropeS", [128, TPC], F32)
    E["ramps_in"] = inp("ramps", [NH, 2, 512], F32)
    E["qkvb_in"] = inp("qkvb", [128, 48], F32)
    E["outb_in"] = inp("outb", [128, 16], F32)
    E["normw_in"] = inp("normw", [128, 44], F32)
    E["m0_in"] = inp("m0", [128, 512], BF16)
    E["m1_in"] = inp("m1", [128, 512], BF16)
    E["mT_in"] = inp("mT", [16, 128, 256], BF16)
    E["identf_in"] = inp("identf", [128, 128], F32)
    E["identb_in"] = inp("identb", [128, 128], BF16)
    E["ones2_in"] = inp("ones2", [2, 128], BF16)
    E["ones1_in"] = inp("ones1", [1, 128], BF16)
    E["ones128_in"] = inp("ones128", [128, 1], BF16)
    E["emask_in"] = inp("emask", [8, 512], BF16)
    E["yout"] = nc.dram_tensor("yout", [TPC, D], F32, kind="ExternalOutput").ap()
    if _DBG is not None:
        E["dbg"] = nc.dram_tensor("dbg", [8 * 1024 * 1024], F32, kind="ExternalOutput").ap()

    E["wsh_bf"] = nc.dram_tensor("wsh_bf", [wtot // NCORE], BF16)
    E["wb"] = nc.dram_tensor("wb", [wtot], BF16)
    E["wpriv"] = nc.dram_tensor("wpriv", [wtot], BF16)
    E["KV0"] = 2 * D * TPC
    E["KV1"] = 2 * 512 * TPC
    E["kv0_in"] = nc.dram_tensor("kv0_in", [E["KV0"]], BF16)
    E["kv0_out"] = nc.dram_tensor("kv0_out", [NCORE * E["KV0"]], BF16)
    E["kv1_in"] = nc.dram_tensor("kv1_in", [E["KV1"]], BF16)
    E["kv1_out"] = nc.dram_tensor("kv1_out", [NCORE * E["KV1"]], BF16)
    E["xn_dram"] = nc.dram_tensor("xn_dram", [TPC, D], BF16)
    E["rcb"] = nc.dram_tensor("rcb", [2 * 128], F32)
    E["stl_a"] = nc.dram_tensor("stl_a", [1 << 19], F32)
    E["stl_b"] = nc.dram_tensor("stl_b", [1 << 19], F32)
    E["stl_ci"] = nc.dram_tensor("stl_ci", [128], F32)
    E["stl_co"] = nc.dram_tensor("stl_co", [NCORE * 128], F32)
    E["ctx_dram"] = nc.dram_tensor("ctx_dram", [D, TPC], BF16)

    with ExitStack() as ctx:
        e = ctx.enter_context
        E["x_res"] = e(nc.sbuf_tensor("x_res", [128, 8 * D], F32))
        E["q_fm"] = e(nc.sbuf_tensor("q_fm", [128, NH * TPC], BF16))
        for nm, shape, dt in (
            ("qkvb", [128, 48], F32), ("outb", [128, 16], F32),
            ("normw", [128, 44], F32), ("m0", [128, 512], BF16),
            ("m1", [128, 512], BF16),
            ("identf", [128, 128], F32), ("identb", [128, 128], BF16),
            ("ones2", [2, 128], BF16), ("ones1", [1, 128], BF16),
            ("ones128", [128, 1], BF16), ("emask", [8, 512], BF16),
            ("ramp2", [2, 512], BF16), ("vec", [128, 32], F32),
            ("row", [1, 1024], F32), ("rowb", [1, 1024], BF16),
            ("gsc", [128, 8], F32), ("colb", [128, 8], BF16),
        ):
            E[nm] = e(nc.sbuf_tensor(nm + "_s", shape, dt))
        E["AR_BYTES"] = 92 * 1024
        E["arena"] = e(nc.sbuf_tensor("arena", [128, E["AR_BYTES"]], mybir.dt.uint8))
        E["ps"] = [e(nc.psum_tensor(f"ps{i}", [128, 512], F32)) for i in range(4)]
        E["psT"] = [e(nc.psum_tensor(f"psT{i}", [128, 1024], BF16)) for i in range(2)]
        E["psA"] = e(nc.psum_tensor("psA", [128, 512], F32))
        sems = {k: e(nc.semaphore(k)) for k in SEMS}
        block = e(nc.Block())

        def run(ename):
            def body(eng):
                E["_free"] = {}
                schedule(Emit(nc, sems, ename, eng), E)
            return body

        block.sync(run("sync"))
        block.gpsimd(run("gpsimd"))
        block.tensor(run("tensor"))
        block.scalar(run("scalar"))
        block.vector(run("vector"))
    return nc


def _w_view(E, name, f, priv=False):
    off, nf, nch, fcols = E["went"][name]
    base = off + f * nch * 128 * fcols
    src_t = E["wpriv"] if priv else E["wb"]
    sl = src_t.ap()[base: base + nch * 128 * fcols]
    return sl.rearrange("(c p f) -> p c f", p=128, f=fcols)


def _kv_kn(E, layer):
    kv_in, kd = (E["kv0_in"], D) if layer == 0 else (E["kv1_in"], 512)
    return kv_in.ap()[0: kd * TPC].rearrange("(a c) -> a c", c=TPC)


def _kv_vtm(E, layer):
    kv_in, kd = (E["kv0_in"], D) if layer == 0 else (E["kv1_in"], 512)
    return kv_in.ap()[kd * TPC: 2 * kd * TPC].rearrange("(t a) -> t a", a=kd)


def _gkn_rank(E, layer, h, r):
    kvout, KV, kd = (E["kv0_out"], E["KV0"], D) if layer == 0 else \
                    (E["kv1_out"], E["KV1"], 512)
    kn = kvout.ap()[r * KV: r * KV + kd * TPC].rearrange("(a c) -> a c", c=TPC)
    return kn[h * 128:(h + 1) * 128, :]


def _gvtm_rank(E, layer, h, r):
    kvout, KV, kd = (E["kv0_out"], E["KV0"], D) if layer == 0 else \
                    (E["kv1_out"], E["KV1"], 512)
    vtm = kvout.ap()[r * KV + kd * TPC: (r + 1) * KV].rearrange("(t a) -> t a", a=kd)
    return vtm[:, h * 128:(h + 1) * 128]



def ag_settled(p, E, in_ap, out_ap, after):
    return p.allgather(in_ap, out_ap, after=after)


def schedule(p, E):
    ar = Arena(E["arena"][:, :], E["AR_BYTES"])
    p.vpad = E["vec"][:, 31:32]
    p.vpad2 = E["vec"][:, 30:31]

    for nm in ("qkvb", "outb", "normw", "m0", "m1",
               "identf", "identb", "ones2", "ones1", "ones128", "emask"):
        p.dma("sync", E[nm][:, :], E[nm + "_in"])
    E["tok_x"] = []
    for tt in range(8):
        E["tok_x"].append(p.dma("sync", E["x_res"][:, tt * D:(tt + 1) * D],
                                E["xin"][tt * 128:(tt + 1) * 128, :]))
    E["ck0"] = p.ckpt("sync")
    tcast = p.dma("gpsimd", E["wsh_bf"].ap(), E["wsh"])
    E["tokW"] = ag_settled(p, E, E["wsh_bf"].ap().opt(), E["wb"].ap().opt(), [tcast])
    E["tok_wpriv"] = []
    for nm in ("w13", "w2", "moeA", "moeB", "wg"):
        off, nf, nch, fcols = E["went"][nm]
        n = nf * nch * 128 * fcols
        E["tok_wpriv"].append(
            p.dma("sync", E["wpriv"].ap()[off:off + n], E["wb"].ap()[off:off + n],
                  after=[E["tokW"]]))

    for layer in (0, 1):
        if f"L{layer}A" in _PARTS or f"L{layer}An" in _PARTS:
            ar.reset()
            phase_A(p, E, ar, layer, ln_only=(f"L{layer}An" in _PARTS))
            p.barrier()
        if f"L{layer}B" in _PARTS:
            ar.reset()
            phase_B(p, E, ar, layer)
            p.barrier()
        if f"L{layer}C" in _PARTS:
            ar.reset()
            phase_C(p, E, ar, layer)
            p.barrier()

    for tt in range(8):
        p.dma("sync", E["yout"][tt * 128:(tt + 1) * 128, :],
              E["x_res"][:, tt * D:(tt + 1) * D])
    if _DBG == "xn":
        p.dma("gpsimd", E["dbg"][0:TPC * D], E["xn_dram"].ap().flatten())
    elif _DBG == "kv0":
        p.dma("gpsimd", E["dbg"][0:E["KV0"]], E["kv0_in"].ap())
    elif _DBG == "kv1":
        p.dma("gpsimd", E["dbg"][0:E["KV1"]], E["kv1_in"].ap())
    elif _DBG == "ctx":
        p.dma("gpsimd", E["dbg"][0:D * TPC], E["ctx_dram"].ap().flatten())
    elif _DBG == "ctxkv":
        p.dma("gpsimd", E["dbg"][0:D * TPC], E["ctx_dram"].ap().flatten())
        p.dma("gpsimd", E["dbg"][D * TPC: D * TPC + D * TPC],
              E["kv0_out"].ap()[0:D * TPC])
        p.dma("gpsimd", E["dbg"][2 * D * TPC: 3 * D * TPC],
              E["kv0_in"].ap()[0: D * TPC])
    elif _DBG == "wb":
        p.dma("gpsimd", E["dbg"][0:8 * 1024 * 1024], E["wb"].ap()[0:8 * 1024 * 1024])
    elif _DBG == "wsh":
        p.dma("gpsimd", E["dbg"][0:E["wsh_bf"].ap().shape[0]], E["wsh_bf"].ap())
    elif _DBG == "q":
        p.dma("gpsimd", E["dbg"][0:NH * TPC * 128].rearrange("(p f) -> p f", p=128),
              E["q_fm"][:, :])
    p.wait("sync", ("LD", p.cnt["LD"]))
    p.wait("gpsimd", ("GD", p.cnt["GD"]))



def phase_A(p, E, ar, layer, ln_only=False):
    ps, psT = E["ps"], E["psT"]
    vec = E["vec"]
    xn_stage = ar.alloc(4 * D, BF16)
    xn_fm = [ar.alloc(16 * 512, BF16), ar.alloc(16 * 512, BF16)]
    lntmp = ar.alloc(D, F32)
    wslab = [ar.alloc(16 * 128, BF16), ar.alloc(16 * 128, BF16)]
    kpush = [ar.alloc(512, BF16), ar.alloc(512, BF16)]
    vstage = ar.alloc(512, BF16)
    kd = D if layer == 0 else 512
    vtm_stage = ar.alloc(4 * kd, BF16)
    qtmp = ar.alloc(512, F32)
    qswap = ar.alloc(512, F32)
    qrope = ar.alloc(512, F32)
    qsq = ar.alloc(512, BF16)
    t1b = ar.alloc(512, F32)
    if layer == 1:
        ropeC = ar.alloc(TPC, F32)
        ropeS = ar.alloc(TPC, F32)
        p.dma("sync", ropeC[:, :], E["ropeC_in"])
        p.dma("sync", ropeS[:, :], E["ropeS_in"])
        trop = [p.ckpt("sync")]

    kn_v = _kv_kn(E, layer)
    vtm_v = _kv_vtm(E, layer)
    kv_in = E["kv0_in"] if layer == 0 else E["kv1_in"]
    kv_out = E["kv0_out"] if layer == 0 else E["kv1_out"]
    push_toks = []

    tok_fm = [None, None]
    tok_stage_free = None
    for g in range(2):
        last_evac = None
        spills = []
        for s4 in range(4):
            tt = g * 4 + s4
            vb = (tt % 2) * 10
            xt = E["x_res"][:, tt * D:(tt + 1) * D]
            ch = bfree(E, "ln_chain")
            ta = p.red(vec[:, vb:vb + 1], xt, ALU.add, after=[E["ck0"], ch])
            tb = p.act(lntmp[:, :], xt, ACT.Square,
                       after=[E["ck0"], ta, bfree(E, "lntmp")])
            tc = p.red(vec[:, vb + 1:vb + 2], lntmp[:, :], ALU.add, after=[tb])
            setfree(E, "lntmp", tc)
            t1 = p.ts(vec[:, vb + 2:vb + 3], vec[:, vb:vb + 1], 1.0 / D, None, ALU.mult,
                 after=[ta, tc])
            t2 = p.ts(vec[:, vb + 3:vb + 4], vec[:, vb + 1:vb + 2], 1.0 / D, None, ALU.mult,
                 after=[tc, t1])
            t3 = p.tt(vec[:, vb + 4:vb + 5], vec[:, vb + 2:vb + 3], vec[:, vb + 2:vb + 3],
                 ALU.mult, after=[t2])
            t4 = p.tt(vec[:, vb + 5:vb + 6], vec[:, vb + 3:vb + 4], vec[:, vb + 4:vb + 5],
                 ALU.subtract, after=[t3])
            tv = p.ts(vec[:, vb + 6:vb + 7], vec[:, vb + 5:vb + 6], LN_EPS, None, ALU.add,
                 after=[t4])
            tsq = p.ssqrt(vec[:, vb + 7:vb + 8], vec[:, vb + 6:vb + 7], after=[tv])
            tr = p.recip(vec[:, vb + 8:vb + 9], vec[:, vb + 7:vb + 8], after=[tsq])
            tnb = p.stt(vec[:, vb + 9:vb + 10], vec[:, vb + 2:vb + 3], -1.0,
                        vec[:, vb + 8:vb + 9], ALU.mult, ALU.mult, after=[tr])
            xn_t = xn_stage[:, s4 * D:(s4 + 1) * D]
            tok_xn = p.act(xn_t, xt, ACT.Identity, bias=vec[:, vb + 9:vb + 10],
                           scale=vec[:, vb + 8:vb + 9],
                           after=[tnb] + (tok_stage_free or []))
            spill = p.dma("sync", E["xn_dram"][tt * 128:(tt + 1) * 128, :], xn_t,
                          after=[tok_xn])
            spills.append(spill)
            setfree(E, "ln_chain", tok_xn)
            for dch in range(16):
                pi = dch % 2
                ttr = p.transpose(psT[pi][:, 0:128], xn_t[:, dch * 128:(dch + 1) * 128],
                                  E["identb"][:, :], after=[tok_xn, bfree(E, f"psT{pi}")])
                last_evac = p.scopy(
                    xn_fm[g][:, dch * 512 + s4 * 128: dch * 512 + (s4 + 1) * 128],
                    psT[pi][:, 0:128], after=[ttr])
                setfree(E, f"psT{pi}", last_evac)
        tok_fm[g] = last_evac
        if _DBG == "ctxkv" and g == 0:
            p.dma("gpsimd", E["dbg"][3 * D * TPC: 3 * D * TPC + 128 * 8192]
                  .rearrange("(p x) -> p x", p=128), xn_fm[0][:, :],
                  after=[last_evac])
        tok_stage_free = [last_evac, p.ckpt("sync", after=spills)]

    if ln_only:
        E[f"tokKV{layer}"] = ag_settled(p, E, kv_in.ap().opt(), kv_out.ap().opt(),
                                        list(push_toks) + [p.ckpt("sync")])
        return
    slab_i = [0]

    def load_slab(wname, f):
        i = slab_i[0] % 2
        slab_i[0] += 1
        p.dma(_SLABQ, wslab[i][:, :], _w_view(E, wname, f),
              after=[bfree(E, f"wslab{i}"), E["tokW"]])
        return i, p.ckpt(_SLABQ)

    def mm16(bank_key, bank_ap, slab, wtok, g, slabkey=None):
        tok = None
        for c in range(16):
            tok = p.mm(bank_ap, slab[:, c * 128:(c + 1) * 128],
                       xn_fm[g][:, c * 512:(c + 1) * 512],
                       start=(c == 0), stop=(c == 15),
                       after=([wtok, bfree(E, bank_key), tok_fm[g]] if c == 0 else []),
                       tag=(c == 15))
        if slabkey is not None:
            setfree(E, slabkey, tok)
        return tok

    def push_k(h, g, bank_ap, bank_key, mmtok, bias_col):
        i = (2 * h + g) % 2
        if bias_col is not None:
            tokp = p.act(kpush[i][:, :], bank_ap, ACT.Identity, bias=bias_col,
                         scale=1.0, after=[mmtok, bfree(E, f"kp{i}")])
        else:
            tokp = p.scopy(kpush[i][:, :], bank_ap, after=[mmtok, bfree(E, f"kp{i}")])
        setfree(E, bank_key, tokp)
        p.dma("sync", kn_v[h * 128:(h + 1) * 128, g * 512:(g + 1) * 512],
              kpush[i][:, :], after=[tokp])
        tokd = p.ckpt("sync")
        setfree(E, f"kp{i}", tokd)
        push_toks.append(tokd)

    def v_evac(f0, h, g, bank_ap, bank_key, mmtok, bias_col):
        if bias_col is not None:
            tokv = p.act(vstage[:, :], bank_ap, ACT.Identity, bias=bias_col, scale=1.0,
                         after=[mmtok, bfree(E, "vstage")])
        else:
            tokv = p.scopy(vstage[:, :], bank_ap, after=[mmtok, bfree(E, "vstage")])
        setfree(E, bank_key, tokv)
        last = None
        for s4 in range(4):
            pi = s4 % 2
            ttr = p.transpose(psT[pi][:, 0:128], vstage[:, s4 * 128:(s4 + 1) * 128],
                              E["identb"][:, :], after=[tokv, bfree(E, f"psT{pi}")])
            last = p.scopy(vtm_stage[:, s4 * kd + h * 128: s4 * kd + (h + 1) * 128],
                           psT[pi][:, 0:128], after=[ttr, bfree(E, "vtm_stage")])
            setfree(E, f"psT{pi}", last)
        setfree(E, "vstage", last)
        return last

    def rope_rms(g, bank_ap, bank_key, mmtok, dest):
        tq = p.scopy(qtmp[:, :], bank_ap, after=[mmtok, bfree(E, "ropebufs")])
        setfree(E, bank_key, tq)
        p.dma("sync", qswap[0:64, :], qtmp[64:128, :], after=[tq])
        p.dma("sync", qswap[64:128, :], qtmp[0:64, :], after=[tq])
        dck = p.ckpt("sync")
        tc = ropeC[:, g * 512:(g + 1) * 512]
        ts2 = ropeS[:, g * 512:(g + 1) * 512]
        m1 = p.tt(t1b[:, :], qtmp[:, :], tc, ALU.mult, after=[tq] + trop)
        m2 = p.tt(qrope[:, :], qswap[:, :], ts2, ALU.mult, after=[dck])
        m3 = p.tt(qrope[:, :], t1b[:, :], qrope[:, :], ALU.add, after=[m1, m2])
        sq = p.act(qsq[:, :], qrope[:, :], ACT.Square, after=[m3])
        tmm = p.mm(E["psA"][0:1, 0:512], E["ones128"][:, :], qsq[:, :],
                   start=True, stop=True, after=[sq, bfree(E, "psA")], tag=True)
        r1 = p.ts(E["row"][0:1, 0:512], E["psA"][0:1, 0:512], 1.0 / 128.0, RMS_EPS,
                  ALU.mult, ALU.add, after=[tmm, bfree(E, "rowA")])
        r2 = p.ssqrt(E["row"][0:1, 512:1024], E["row"][0:1, 0:512], after=[r1])
        r3 = p.recip(E["row"][0:1, 0:512], E["row"][0:1, 512:1024], after=[r2])
        r4 = p.scopy(E["rowb"][0:1, 0:512], E["row"][0:1, 0:512],
                     after=[r3, bfree(E, "rowb")])
        setfree(E, "rowA", r4)
        tbc = p.mm(E["psA"][:, 0:512], E["ones1"][:, :], E["rowb"][0:1, 0:512],
                   start=True, stop=True, after=[r4, r1], tag=True)
        setfree(E, "rowb", tbc)
        tqn = p.tt(dest, E["psA"][:, 0:512], qrope[:, :], ALU.mult, after=[tbc])
        setfree(E, "psA", tqn)
        setfree(E, "ropebufs", tqn)
        return tqn

    rot = [0]
    def nextbank():
        b = rot[0] % 4
        rot[0] += 1
        return f"ps{b}", ps[b][:, :]

    if layer == 0:
        nf_qk = (16 if "q" in _PROJ else 0) + (16 if "k" in _PROJ else 0)
        for f in range(nf_qk):
            si, wtok = load_slab("qkv", f)
            for g in range(2):
                bk, ba = nextbank()
                tok = mm16(bk, ba, wslab[si], wtok, g,
                           slabkey=(f"wslab{si}" if g == 1 else None))
                if f < 16:
                    tokf = p.act(E["q_fm"][:, f * TPC + g * 512: f * TPC + (g + 1) * 512],
                                 ba, ACT.Identity, bias=E["qkvb"][:, f:f + 1], scale=1.0,
                                 after=[tok])
                    setfree(E, bk, tokf)
                else:
                    push_k(f - 16, g, ba, bk, tok, E["qkvb"][:, f:f + 1])
        for g in range(2):
            for f in (range(32, 48) if "v" in _PROJ else []):
                si, wtok = load_slab("qkv", f)
                bk, ba = nextbank()
                tok = mm16(bk, ba, wslab[si], wtok, g, slabkey=f"wslab{si}")
                v_evac(32, f - 32, g, ba, bk, tok, E["qkvb"][:, f:f + 1])
            if "v" in _PROJ:
                for s4 in range(4):
                    p.dma("sync", vtm_v[g * 512 + s4 * 128: g * 512 + (s4 + 1) * 128, :],
                          vtm_stage[:, s4 * kd:(s4 + 1) * kd],
                          after=[bfree(E, "vstage")])
                tokd = p.ckpt("sync")
                push_toks.append(tokd)
                setfree(E, "vtm_stage", tokd)
    else:
        for f in range(16):
            si, wtok = load_slab("wq", f)
            for g in range(2):
                bk, ba = nextbank()
                tok = mm16(bk, ba, wslab[si], wtok, g,
                           slabkey=(f"wslab{si}" if g == 1 else None))
                rope_rms(g, ba, bk, tok,
                         E["q_fm"][:, f * TPC + g * 512: f * TPC + (g + 1) * 512])
        for f in range(4):
            si, wtok = load_slab("wk", f)
            for g in range(2):
                bk, ba = nextbank()
                tok = mm16(bk, ba, wslab[si], wtok, g,
                           slabkey=(f"wslab{si}" if g == 1 else None))
                tkn = rope_rms(g, ba, bk, tok, kpush[0][:, :])
                p.dma("sync", kn_v[f * 128:(f + 1) * 128, g * 512:(g + 1) * 512],
                      kpush[0][:, :], after=[tkn])
                tokd = p.ckpt("sync")
                setfree(E, "ropebufs", tokd)
                push_toks.append(tokd)
        for g in range(2):
            for f in range(4):
                si, wtok = load_slab("wv", f)
                bk, ba = nextbank()
                tok = mm16(bk, ba, wslab[si], wtok, g, slabkey=f"wslab{si}")
                v_evac(0, f, g, ba, bk, tok, None)
            for s4 in range(4):
                tokd = p.dma("sync", vtm_v[g * 512 + s4 * 128: g * 512 + (s4 + 1) * 128, :],
                             vtm_stage[:, s4 * kd:(s4 + 1) * kd],
                             after=[bfree(E, "vstage")])
                push_toks.append(tokd)
            setfree(E, "vtm_stage", push_toks[-1])

    E[f"tokKV{layer}"] = ag_settled(p, E, kv_in.ap().opt(), kv_out.ap().opt(),
                                    list(push_toks) + [p.ckpt("sync")])



def phase_B(p, E, ar, layer):
    if layer == 0:
        ffn0(p, E, ar)
    else:
        moe1(p, E, ar)


def ffn0(p, E, ar):
    ps, psT, psA = E["ps"], E["psT"], E["psA"]
    vec = E["vec"]
    xn_ch = ar.alloc(2 * D, BF16)
    xq_fm = ar.alloc(16 * 256, BF16)
    z_tm = ar.alloc(2 * FFH, BF16)
    iq_fm = ar.alloc(NHT * 256, BF16)
    m0 = ar.mark()

    chunk_guard = None
    for tc in range(4):
        ar.reset(m0)
        qtmp = ar.alloc(D, F32)
        ar.reset(m0)
        w13s = [ar.alloc(16 * 128, BF16) for _ in range(4)]
        a_st = ar.alloc(256, F32)
        g2_st = ar.alloc(256, F32)
        g_st = ar.alloc(256, BF16)
        gg_st = ar.alloc(256, BF16)
        z_st = ar.alloc(256, BF16)
        ar.reset(m0)
        ztmp = ar.alloc(1024, F32)
        ar.reset(m0)
        w2s = [ar.alloc(NHT * 128, BF16), ar.alloc(NHT * 128, BF16)]
        dtmp2 = ar.alloc(256, F32)

        tt0 = tc * 2
        for k in range(2):
            p.dma("sync", xn_ch[:, k * D:(k + 1) * D],
                  E["xn_dram"][(tt0 + k) * 128:(tt0 + k + 1) * 128, :],
                  after=[bfree(E, "xn_ch"), chunk_guard])
        tok_xn = [p.ckpt("sync")] * 2

        tok_xq = None
        for k in range(2):
            vb = k * 10
            xn_t = xn_ch[:, k * D:(k + 1) * D]
            tm = p.red(vec[:, vb:vb + 1], xn_t, ALU.max, absval=True, after=[tok_xn[k]])
            tm2 = p.ts(vec[:, vb + 1:vb + 2], vec[:, vb:vb + 1], 1e-5, None, ALU.max,
                       after=[tm])
            tre = p.recip(vec[:, vb + 2:vb + 3], vec[:, vb + 1:vb + 2], after=[tm2])
            tsc = p.ts(vec[:, vb + 3:vb + 4], vec[:, vb + 2:vb + 3], 127.0, None,
                       ALU.mult, after=[tre])
            tiv = p.recip(vec[:, vb + 4:vb + 5], vec[:, vb + 3:vb + 4], after=[tsc])
            ty = p.act(qtmp[:, :], xn_t, ACT.Identity, bias=0.0,
                       scale=vec[:, vb + 3:vb + 4], after=[tsc])
            tyr = p.ts(qtmp[:, :], qtmp[:, :], MAGIC, -MAGIC, ALU.add, ALU.add,
                       after=[ty])
            tyq = p.ts(qtmp[:, :], qtmp[:, :], vec[:, vb + 4:vb + 5], None, ALU.mult,
                       after=[tyr, tiv])
            for c in range(16):
                b = 2 + c % 2
                ttr = p.transpose(ps[b][:, 0:128].bitcast(F32),
                                  qtmp[:, c * 128:(c + 1) * 128],
                                  E["identf"][:, :], after=[tyq, bfree(E, f"ps{b}")])
                tok_xq = p.scopy(xq_fm[:, c * 256 + k * 128: c * 256 + (k + 1) * 128],
                                 ps[b][:, 0:128], after=[ttr])
                setfree(E, f"ps{b}", tok_xq)
        setfree(E, "xn_ch", tok_xq)

        tok_ztm = None
        stats_guard = bfree(E, "psA")
        for i in range(NHT):
            ia, ib = (2 * i) % 4, (2 * i + 1) % 4
            p.dma("sync", w13s[ia][:, :], _w_view(E, "w13", i, priv=True),
                  after=[bfree(E, f"w13s{ia}"), E["tok_wpriv"], tok_xq])
            p.dma("sync", w13s[ib][:, :], _w_view(E, "w13", NHT + i, priv=True),
                  after=[bfree(E, f"w13s{ib}"), E["tok_wpriv"], tok_xq])
            wta = wtb = p.ckpt("sync")
            bA, bB = i % 2, 2 + i % 2
            tokA = tokB = None
            for c in range(16):
                tokA = p.mm(ps[bA][:, 0:256], w13s[ia][:, c * 128:(c + 1) * 128],
                            xq_fm[:, c * 256:(c + 1) * 256],
                            start=(c == 0), stop=(c == 15),
                            after=([wta, bfree(E, f"ps{bA}"), tok_xq] if c == 0 else []),
                            tag=(c == 15))
            for c in range(16):
                tokB = p.mm(ps[bB][:, 0:256], w13s[ib][:, c * 128:(c + 1) * 128],
                            xq_fm[:, c * 256:(c + 1) * 256],
                            start=(c == 0), stop=(c == 15),
                            after=([wtb, bfree(E, f"ps{bB}")] if c == 0 else []),
                            tag=(c == 15))
            setfree(E, f"w13s{ia}", tokA)
            setfree(E, f"w13s{ib}", tokB)
            ta1 = p.scopy(a_st[:, :], ps[bA][:, 0:256], after=[tokA, bfree(E, "gbufs")])
            setfree(E, f"ps{bA}", ta1)
            tg2 = p.stt(g2_st[:, :], a_st[:, :], 0.0, a_st[:, :], ALU.max, ALU.mult,
                        after=[ta1])
            tg = p.tt(g_st[:, :], ps[bB][:, 0:256], g2_st[:, :], ALU.mult,
                      after=[tg2, tokB])
            setfree(E, f"ps{bB}", tg)
            tgg = p.act(gg_st[:, :], g_st[:, :], ACT.Square, after=[tg])
            p.mm(psA[0:1, 0:256], E["ones128"][:, :], gg_st[:, :],
                 start=(i == 0), stop=(i == NHT - 1),
                 after=([tgg, stats_guard] if i == 0 else [tgg]),
                 tag=(i == NHT - 1))
            tz = p.ts(z_st[:, :], g_st[:, :], E["normw"][:, i:i + 1], None, ALU.mult,
                      after=[tg])
            gfree = [tgg, tz]
            for u in range(2):
                pi = u % 2
                ttr = p.transpose(psT[pi][:, 0:128], z_st[:, u * 128:(u + 1) * 128],
                                  E["identb"][:, :], after=[tz, bfree(E, f"psT{pi}")])
                tok_ztm = p.scopy(z_tm[:, u * FFH + i * 128: u * FFH + (i + 1) * 128],
                                  psT[pi][:, 0:128], after=[ttr])
                setfree(E, f"psT{pi}", tok_ztm)
            gfree.append(tok_ztm)
            setfree(E, "gbufs", *gfree)
        if _DBG == "ztm":
            p.dma("gpsimd", E["dbg"][tc * 128 * 2 * FFH: (tc + 1) * 128 * 2 * FFH]
                  .rearrange("(p x) -> p x", p=128), z_tm[:, :], after=[tok_ztm])
        tok_stats = ("T", p.cnt["T"])

        sr = p.scopy(E["row"][0:1, 0:256], psA[0:1, 0:256],
                     after=[tok_stats, bfree(E, "rowA")])
        setfree(E, "psA", sr)
        r1 = p.ts(E["row"][0:1, 256:512], E["row"][0:1, 0:256], 1.0 / FFH, FFN_EPS,
                  ALU.mult, ALU.add, after=[sr])
        r2 = p.ssqrt(E["row"][0:1, 512:768], E["row"][0:1, 256:512], after=[r1])
        r3 = p.recip(E["row"][0:1, 0:256], E["row"][0:1, 512:768], after=[r2])
        setfree(E, "rowA", r3)
        for k in range(2):
            rb = p.dma("sync", E["rcb"].ap()[k * 128:(k + 1) * 128]
                       .rearrange("(o p) -> o p", o=1),
                       E["row"][0:1, k * 128:(k + 1) * 128], after=[r3])
            p.dma("sync", vec[:, 20 + k:21 + k],
                  E["rcb"].ap()[k * 128:(k + 1) * 128]
                  .rearrange("(p o) -> p o", o=1), after=[rb])
        rcol = [p.ckpt("sync")] * 2

        tok_iq = None
        invs = {}
        for k in range(2):
            vb = k * 10
            zt = z_tm[:, k * FFH:(k + 1) * FFH]
            tmz = p.red(vec[:, vb:vb + 1], zt, ALU.max, absval=True, after=[tok_ztm])
            tmi = p.tt(vec[:, vb + 1:vb + 2], vec[:, 20 + k:21 + k], vec[:, vb:vb + 1],
                       ALU.mult, after=[tmz, rcol[k]])
            tmc = p.ts(vec[:, vb + 2:vb + 3], vec[:, vb + 1:vb + 2], 1e-5, None, ALU.max,
                       after=[tmi])
            tsr = p.recip(vec[:, vb + 3:vb + 4], vec[:, vb + 2:vb + 3], after=[tmc])
            tss = p.ts(vec[:, vb + 4:vb + 5], vec[:, vb + 3:vb + 4], 127.0, None,
                       ALU.mult, after=[tsr])
            tvi = p.recip(vec[:, vb + 5:vb + 6], vec[:, vb + 4:vb + 5], after=[tss])
            invs[k] = (vec[:, vb + 5:vb + 6], tvi)
            tF = p.tt(vec[:, vb + 6:vb + 7], vec[:, 20 + k:21 + k],
                      vec[:, vb + 4:vb + 5], ALU.mult, after=[tss])
            for hs in range(6):
                w = min(1024, FFH - hs * 1024)
                ty = p.ts(ztmp[:, 0:w], zt[:, hs * 1024: hs * 1024 + w],
                          vec[:, vb + 6:vb + 7], None, ALU.mult,
                          after=[tF, bfree(E, "ztmp")])
                tyr = p.ts(ztmp[:, 0:w], ztmp[:, 0:w], MAGIC, -MAGIC, ALU.add, ALU.add,
                           after=[ty])
                for u in range(w // 128):
                    hc = hs * 8 + u
                    b = 2 + u % 2
                    ttr = p.transpose(ps[b][:, 0:128].bitcast(F32),
                                      ztmp[:, u * 128:(u + 1) * 128],
                                      E["identf"][:, :], after=[tyr, bfree(E, f"ps{b}")])
                    tok_iq = p.scopy(iq_fm[:, hc * 256 + k * 128: hc * 256 + (k + 1) * 128],
                                     ps[b][:, 0:128], after=[ttr])
                    setfree(E, f"ps{b}", tok_iq)
                setfree(E, "ztmp", tok_iq)

        if _DBG == "iq":
            p.dma("gpsimd", E["dbg"][tc * 128 * NHT * 256: (tc + 1) * 128 * NHT * 256]
                  .rearrange("(p x) -> p x", p=128), iq_fm[:, :], after=[tok_iq])
        if _DBG == "xq":
            p.dma("gpsimd", E["dbg"][tc * 128 * 16 * 256: (tc + 1) * 128 * 16 * 256]
                  .rearrange("(p x) -> p x", p=128), xq_fm[:, :], after=[tok_xq])
        for ft in range(16):
            iw = ft % 2
            p.dma("sync", w2s[iw][:, :], _w_view(E, "w2", ft, priv=True),
                  after=[bfree(E, f"w2s{iw}"), E["tok_wpriv"], tok_iq])
            wt = p.ckpt("sync")
            b = ft % 2
            tok = None
            for hc in range(NHT):
                tok = p.mm(ps[b][:, 0:256], w2s[iw][:, hc * 128:(hc + 1) * 128],
                           iq_fm[:, hc * 256:(hc + 1) * 256],
                           start=(hc == 0), stop=(hc == NHT - 1),
                           after=([wt, bfree(E, f"ps{b}"), tok_iq] if hc == 0 else []),
                           tag=(hc == NHT - 1))
            setfree(E, f"w2s{iw}", tok)
            td = p.scopy(dtmp2[:, :], ps[b][:, 0:256], after=[tok, bfree(E, "dtmp")])
            setfree(E, f"ps{b}", td)
            last_add = None
            for u in range(2):
                bb = 2 + u % 2
                ttr = p.transpose(ps[bb][:, 0:128].bitcast(F32),
                                  dtmp2[:, u * 128:(u + 1) * 128], E["identf"][:, :],
                                  after=[td, bfree(E, f"ps{bb}")])
                xsl = E["x_res"][:, (tt0 + u) * D + ft * 128:(tt0 + u) * D + (ft + 1) * 128]
                last_add = p.stt(xsl, ps[bb][:, 0:128], invs[u][0], xsl,
                                 ALU.mult, ALU.add, after=[ttr, invs[u][1]])
                setfree(E, f"ps{bb}", last_add)
            setfree(E, "dtmp", last_add)
            chunk_guard = last_add


def moe1(p, E, ar):
    ps, psT, psA = E["ps"], E["psT"], E["psA"]
    vec = E["vec"]
    xn_ch = ar.alloc(4 * D, BF16)
    xn_fm = ar.alloc(16 * 512, BF16)
    wgs = ar.alloc(16 * 8, BF16)
    Aslab = [ar.alloc(16 * 128, BF16), ar.alloc(16 * 128, BF16)]
    Bslab = [ar.alloc(4 * 128, BF16), ar.alloc(4 * 128, BF16)]
    Hm_st = ar.alloc(4 * 512, BF16)
    mex_st = ar.alloc(512, BF16)
    mask_fm = ar.alloc(512, BF16)
    lgjunk = ar.alloc(8, F32)
    dtmp = ar.alloc(512, F32)

    p.dma("sync", wgs, _w_view(E, "wg", 0, priv=True),
          after=[E["tok_wpriv"]])
    twg = p.ckpt("sync")

    for g in range(2):
        for k in range(4):
            p.dma("sync", xn_ch[:, k * D:(k + 1) * D],
                  E["xn_dram"][(g * 4 + k) * 128:(g * 4 + k + 1) * 128, :],
                  after=[bfree(E, "xn_ch")])
        tok_xn = [p.ckpt("sync")] * 4
        tok_fm = None
        for k in range(4):
            for dch in range(16):
                pi = dch % 2
                ttr = p.transpose(psT[pi][:, 0:128],
                                  xn_ch[:, k * D + dch * 128: k * D + (dch + 1) * 128],
                                  E["identb"][:, :], after=[tok_xn[k], bfree(E, f"psT{pi}")])
                tok_fm = p.scopy(xn_fm[:, dch * 512 + k * 128: dch * 512 + (k + 1) * 128],
                                 psT[pi][:, 0:128], after=[ttr])
                setfree(E, f"psT{pi}", tok_fm)
        setfree(E, "xn_ch", tok_fm)

        tok_mask = None
        for k in range(4):
            b = 2 + k % 2
            tok = None
            for c in range(16):
                tok = p.mm(ps[b][:, 0:8],
                           xn_fm[:, c * 512 + k * 128: c * 512 + (k + 1) * 128],
                           wgs[:, c * 8:(c + 1) * 8],
                           start=(c == 0), stop=(c == 15),
                           after=([twg, bfree(E, f"ps{b}"), tok_fm] if c == 0 else []),
                           tag=(c == 15))
            vb = (k % 2) * 10
            tn = p.red(vec[:, vb:vb + 1], ps[b][:, 0:8], ALU.max, negate=True,
                       after=[tok])
            tmk = p.ts(E["colb"][:, 0:8], ps[b][:, 0:8], vec[:, vb:vb + 1], 0.0,
                       ALU.add, ALU.is_ge, after=[tn, bfree(E, "colb")])
            te = p.act(lgjunk[:, :], ps[b][:, 0:8], ACT.Exp, bias=vec[:, vb:vb + 1],
                       accum_out=vec[:, vb + 1:vb + 2], after=[tn])
            setfree(E, f"ps{b}", te, tmk)
            tr = p.recip(vec[:, vb + 2:vb + 3], vec[:, vb + 1:vb + 2], after=[te])
            p.ts(E["gsc"][:, g * 4 + k: g * 4 + k + 1], vec[:, vb + 2:vb + 3], 4.0, None,
                 ALU.mult, after=[tr])
            ttr = p.transpose(psT[0][0:8, 0:128], E["colb"][:, 0:8], E["identb"][:, :],
                              after=[tmk, bfree(E, "psT0")])
            setfree(E, "colb", ttr)
            tok_mask = p.scopy(mask_fm[0:8, k * 128:(k + 1) * 128], psT[0][0:8, 0:128],
                               after=[ttr])
            setfree(E, "psT0", tok_mask)

        tok_hm = None
        for et in range(4):
            ia = et % 2
            p.dma("sync", Aslab[ia][:, :], _w_view(E, "moeA", et, priv=True),
                  after=[bfree(E, f"Aslab{ia}"), E["tok_wpriv"]])
            wt = p.ckpt("sync")
            bH = et % 2
            tokH = None
            for c in range(16):
                tokH = p.mm(ps[bH][:, 0:512], Aslab[ia][:, c * 128:(c + 1) * 128],
                            xn_fm[:, c * 512:(c + 1) * 512],
                            start=(c == 0), stop=(c == 15),
                            after=([wt, bfree(E, f"ps{bH}"), tok_fm] if c == 0 else []),
                            tag=(c == 15))
            setfree(E, f"Aslab{ia}", tokH)
            bM = 2 + et % 2
            tokM = p.mm(ps[bM][:, 0:512], E["emask"][:, et * 128:(et + 1) * 128],
                        mask_fm[0:8, 0:512], start=True, stop=True,
                        after=[tok_mask, bfree(E, f"ps{bM}")], tag=True)
            tme = p.scopy(mex_st[:, :], ps[bM][:, 0:512], after=[tokM, bfree(E, "mex")])
            setfree(E, f"ps{bM}", tme)
            tok_hm = p.tt(Hm_st[:, et * 512:(et + 1) * 512], ps[bH][:, 0:512],
                          mex_st[:, :], ALU.mult, after=[tme, tokH])
            setfree(E, f"ps{bH}", tok_hm)
            setfree(E, "mex", tok_hm)

        for ft in range(16):
            ib = ft % 2
            p.dma("sync", Bslab[ib][:, :], _w_view(E, "moeB", ft, priv=True),
                  after=[bfree(E, f"Bslab{ib}"), E["tok_wpriv"]])
            wt = p.ckpt("sync")
            b = ft % 2
            tok = None
            for ec in range(4):
                tok = p.mm(ps[b][:, 0:512], Bslab[ib][:, ec * 128:(ec + 1) * 128],
                           Hm_st[:, ec * 512:(ec + 1) * 512],
                           start=(ec == 0), stop=(ec == 3),
                           after=([wt, bfree(E, f"ps{b}"), tok_hm] if ec == 0 else []),
                           tag=(ec == 3))
            setfree(E, f"Bslab{ib}", tok)
            td = p.scopy(dtmp[:, :], ps[b][:, 0:512], after=[tok, bfree(E, "dtmp")])
            setfree(E, f"ps{b}", td)
            last_add = None
            for u in range(4):
                bb = 2 + u % 2
                ttr = p.transpose(ps[bb][:, 0:128].bitcast(F32),
                                  dtmp[:, u * 128:(u + 1) * 128], E["identf"][:, :],
                                  after=[td, bfree(E, f"ps{bb}")])
                tt_ = g * 4 + u
                xsl = E["x_res"][:, tt_ * D + ft * 128: tt_ * D + (ft + 1) * 128]
                last_add = p.stt(xsl, ps[bb][:, 0:128], E["gsc"][:, tt_:tt_ + 1], xsl,
                                 ALU.mult, ALU.add, after=[ttr])
                setfree(E, f"ps{bb}", last_add)
            setfree(E, "dtmp", last_add)



def _load_kv_head(p, E, layer, h, kbuf, vbuf, nkt, guard):
    ktoks, vtoks = [], []
    for r in range(NCORE):
        kv = kbuf[:, :].rearrange("q (kt rr l) -> q kt rr l", rr=8, l=64)[
            :, 0:nkt, r:r + 1, :].squeeze(2)
        src = _gkn_rank(E, layer, h, r)[:, 0:nkt * 64].rearrange(
            "q (kt l) -> q kt l", l=64)
        p.dma("sync", kv, src, after=[E[f"tokKV{layer}"], guard])
        half = r % 2
        vv = vbuf[64 * half:64 * half + 64, :].rearrange(
            "l (kt u d) -> l kt u d", u=4, d=128)[:, :, (r // 2):(r // 2) + 1, :].squeeze(2)
        vsrc = _gvtm_rank(E, layer, h, r)[0:nkt * 64, :].rearrange(
            "(kt l) d -> l kt d", l=64)
        p.dma("sync", vv, vsrc, after=[E[f"tokKV{layer}"], guard])
    ck = [p.ckpt("sync")]
    return ck, ck


def attn0(p, E, ar):
    ps, psT = E["ps"], E["psT"]
    vec = E["vec"]
    kbuf = ar.alloc(16 * 512, BF16)
    vbuf = ar.alloc(64 * 128, BF16)
    Pst = ar.alloc(512, BF16)
    PTst = [ar.alloc(128, BF16), ar.alloc(128, BF16)]
    bcast = ar.alloc(128, BF16)
    ctxsb = ar.alloc(128, BF16)
    dstage = ar.alloc(128, F32) if _DBG == "ctxps" else None
    E["p2stage"] = ar.alloc(1024, F32) if _DBG == "p2" else None
    E["_p2tok"] = None
    E["att_mark"] = ar.mark()

    prev_tok = None
    for h in range(NH):
        p.dma("gpsimd", E["ramp2"][:, :], E["ramps_in"][h, :, :],
              after=[bfree(E, "ramp2"), prev_tok])
        tok_ramp = p.ckpt("gpsimd")
        ktoks, vtoks = _load_kv_head(p, E, 0, h, kbuf, vbuf, 16, prev_tok)
        if _DBG == "kbuf" and h == 0:
            p.dma("gpsimd", E["dbg"][0:128 * 8192].rearrange("(p x) -> p x", p=128),
                  kbuf[:, :], after=ktoks + vtoks)
            p.dma("gpsimd", E["dbg"][128 * 8192:2 * 128 * 8192]
                  .rearrange("(p x) -> p x", p=128), vbuf[:, :], after=ktoks + vtoks)
        for qi in range(8):
            A = _kcut(h, qi)
            ntile = 2 * qi + 2
            lhsq = E["q_fm"][:, h * TPC + qi * 128: h * TPC + (qi + 1) * 128]
            mcol = vec[:, 4 + (qi % 4):5 + (qi % 4)]
            first = True
            for kt in range(A, ntile):
                b = kt % 2
                p.mm(ps[b][:, :], lhsq, kbuf[:, kt * 512:(kt + 1) * 512],
                     start=True, stop=False,
                     after=[ktoks, bfree(E, f"ps{b}")])
                last = (kt < 2 * qi)
                tok = p.mm(ps[b][:, :], E["ones2"][:, :], E["ramp2"][:, :],
                           start=False, stop=last, after=[tok_ramp], tag=last)
                if kt == 2 * qi:
                    tok = p.mm(ps[b][:, :], E["identb"][:, :], E["m0"][:, :],
                               start=False, stop=True, tag=True)
                elif kt == 2 * qi + 1:
                    tok = p.mm(ps[b][:, :], E["identb"][:, :], E["m1"][:, :],
                               start=False, stop=True, tag=True)
                tmax = p.red(vec[:, 0:1], ps[b][:, :], ALU.max,
                             after=[tok, bfree(E, "v0r")])
                setfree(E, f"ps{b}", tmax)
                gg = ALIBI_SLOPES[h] * float(512 * kt)
                if first:
                    tm = p.ts(mcol, vec[:, 0:1], gg, None, ALU.add, after=[tmax])
                    first = False
                else:
                    tm = p.stt(mcol, vec[:, 0:1], gg, mcol, ALU.add, ALU.max,
                               after=[tmax])
                setfree(E, "v0r", tm)
            if _DBG == "mrun" and h == 0:
                p.dma("gpsimd", E["dbg"][qi * 128:(qi + 1) * 128]
                      .rearrange("(p o) -> p o", o=1), mcol, after=[tm])
            cb = 2 + qi % 2
            denom = vec[:, 2:3]
            ctx_first = True
            tok_ctx = None
            tden = None
            for kt in range(A, ntile):
                b = kt % 2
                p.mm(ps[b][:, :], lhsq, kbuf[:, kt * 512:(kt + 1) * 512],
                     start=True, stop=False, after=[bfree(E, f"ps{b}")])
                last = (kt < 2 * qi)
                tok = p.mm(ps[b][:, :], E["ones2"][:, :], E["ramp2"][:, :],
                           start=False, stop=last, tag=last)
                if kt == 2 * qi:
                    tok = p.mm(ps[b][:, :], E["identb"][:, :], E["m0"][:, :],
                               start=False, stop=True, tag=True)
                elif kt == 2 * qi + 1:
                    tok = p.mm(ps[b][:, :], E["identb"][:, :], E["m1"][:, :],
                               start=False, stop=True, tag=True)
                gg = ALIBI_SLOPES[h] * float(512 * kt)
                if _DBG == "p2" and h == 0 and qi == 0 and kt == A:
                    tsnap = p.vcopy(vec[:, 10:11], mcol)
                    p.dma("sync", E["dbg"][262144:262272]
                          .rearrange("(p o) -> p o", o=1), vec[:, 10:11],
                          after=[tsnap])
                tmb = p.ts(vec[:, 3:4], mcol, -1.0, gg, ALU.mult, ALU.add,
                           after=[bfree(E, "mb")])
                texp = p.act(Pst[:, :], ps[b][:, :], ACT.Exp, bias=vec[:, 3:4],
                             accum_out=vec[:, 0:1],
                             after=[tok, tmb, bfree(E, "pst"), bfree(E, "acc")])
                setfree(E, "mb", texp)
                setfree(E, f"ps{b}", texp)
                if _DBG == "p2" and h == 0 and qi == 0:
                    tsc = p.scopy(E["p2stage"][:, kt * 512:(kt + 1) * 512],
                                  ps[b][:, :], after=[texp])
                    p.dma("sync", E["dbg"][kt * 65536:(kt + 1) * 65536]
                          .rearrange("(p x) -> p x", p=128),
                          E["p2stage"][:, kt * 512:(kt + 1) * 512], after=[tsc])
                    tpd = p.dma("gpsimd", E["dbg"][131072 + kt * 65536:
                                                   131072 + (kt + 1) * 65536]
                                .rearrange("(p x) -> p x", p=128), Pst[:, :],
                                after=[texp])
                    setfree(E, f"ps{b}", texp, tsc)
                    E["_p2tok"] = tpd
                if kt == A:
                    tden = p.vcopy(denom, vec[:, 0:1], after=[texp])
                else:
                    tden = p.tt(denom, denom, vec[:, 0:1], ALU.add, after=[texp])
                setfree(E, "acc", tden)
                pst_readers = []
                for u in range(4):
                    pi = u % 2
                    ttr = p.transpose(psT[pi][:, 0:128], Pst[:, u * 128:(u + 1) * 128],
                                      E["identb"][:, :],
                                      after=[texp, bfree(E, f"psT{pi}")])
                    pst_readers.append(ttr)
                    ti = u % 2
                    tpt = p.scopy(PTst[ti][:, :], psT[pi][:, 0:128],
                                  after=[ttr, bfree(E, f"PT{ti}")])
                    setfree(E, f"psT{pi}", tpt)
                    tok_ctx = p.mm(ps[cb][:, 0:128],
                                   vbuf[:, (kt * 4 + u) * 128:(kt * 4 + u + 1) * 128],
                                   PTst[ti][:, :],
                                   start=ctx_first, stop=(kt == ntile - 1 and u == 3),
                                   after=[tpt, vtoks] +
                                         ([bfree(E, f"ps{cb}")] if ctx_first else []),
                                   tag=True)
                    setfree(E, f"PT{ti}", tok_ctx)
                    ctx_first = False
                setfree(E, "pst", pst_readers[-1], E.get("_p2tok"))
                E["_p2tok"] = None
            if _DBG == "ctxps" and h == 0:
                tds = p.scopy(dstage[:, :], ps[cb][:, 0:128], after=[tok_ctx])
                p.dma("gpsimd", E["dbg"][qi * 128 * 128:(qi + 1) * 128 * 128]
                      .rearrange("(p x) -> p x", p=128), dstage[:, :], after=[tds])
                setfree(E, f"ps{cb}", tds)
            if _DBG == "den" and h == 0:
                p.dma("gpsimd", E["dbg"][qi * 128:(qi + 1) * 128]
                      .rearrange("(p o) -> p o", o=1), denom, after=[tden])
            trc = p.recip(vec[:, 9:10], denom, after=[tden])
            tcb = p.scopy(E["colb"][:, 1:2], vec[:, 9:10], after=[trc, bfree(E, "colb1")])
            ttr = p.transpose(psT[0][0:1, 0:128], E["colb"][:, 1:2], E["identb"][:, :],
                              after=[tcb, bfree(E, "psT0")])
            setfree(E, "colb1", ttr)
            trow = p.scopy(E["rowb"][0:1, 0:128], psT[0][0:1, 0:128],
                           after=[ttr, bfree(E, "rowb")])
            setfree(E, "psT0", trow)
            tbc = p.mm(E["psA"][:, 0:128], E["ones1"][:, :], E["rowb"][0:1, 0:128],
                       start=True, stop=True, after=[trow, bfree(E, "psA")], tag=True)
            setfree(E, "rowb", tbc)
            tb2 = p.scopy(bcast[:, :], E["psA"][:, 0:128], after=[tbc, bfree(E, "bcast")])
            setfree(E, "psA", tb2)
            tcx = p.tt(ctxsb[:, :], ps[cb][:, 0:128], bcast[:, :], ALU.mult,
                       after=[tb2, tok_ctx, bfree(E, "ctxsb")])
            setfree(E, f"ps{cb}", tcx)
            setfree(E, "bcast", tcx)
            p.dma("sync", E["ctx_dram"].ap()[h * 128:(h + 1) * 128,
                                             qi * 128:(qi + 1) * 128],
                  ctxsb[:, :], after=[tcx])
            tsp = p.ckpt("sync")
            setfree(E, "ctxsb", tsp)
            prev_tok = tsp
        setfree(E, "ramp2", prev_tok)


def attn1(p, E, ar):
    ps, psT, psA = E["ps"], E["psT"], E["psA"]
    kbuf = ar.alloc(16 * 512, BF16)
    vbuf = ar.alloc(64 * 128, BF16)
    PTst = ar.alloc(256, BF16)
    bcast = ar.alloc(256, BF16)
    ctxsb = ar.alloc(256, BF16)
    mT = ar.alloc(16 * 256, BF16)
    p.dma("sync", mT[:, :], E["mT_in"].rearrange("a p q -> p a q"))
    tmT = p.ckpt("sync")
    cstb = E["vec"][:, 24:25]
    tcst = p.vmemset(cstb, -L1_CST)
    E["att_mark"] = ar.mark()

    prev_tok = None
    for g in range(4):
        ktoks, vtoks = _load_kv_head(p, E, 1, g, kbuf, vbuf, 16, prev_tok)
        for h in range(4 * g, 4 * g + 4):
            for qg in range(4):
                qcols = E["q_fm"][:, h * TPC + qg * 256: h * TPC + (qg + 1) * 256]
                nch = 16 * qg + 16
                cb = 2 + qg % 2
                tok_ctx = tok_den = None
                for m in range(nch):
                    b = m % 2
                    reg = m - 16 * qg
                    if reg < 0:
                        tok = p.mm(ps[b][:, 0:256], kbuf[:, m * 128:(m + 1) * 128],
                                   qcols, start=True, stop=True,
                                   after=[ktoks, bfree(E, f"ps{b}")], tag=True)
                    else:
                        p.mm(ps[b][:, 0:256], kbuf[:, m * 128:(m + 1) * 128],
                             qcols, start=True, stop=False,
                             after=[ktoks, bfree(E, f"ps{b}")])
                        tok = p.mm(ps[b][:, 0:256], E["identb"][:, :],
                                   mT[:, reg * 256:(reg + 1) * 256],
                                   start=False, stop=True,
                                   after=[tmT], tag=True)
                    texp = p.act(PTst[:, :], ps[b][:, 0:256], ACT.Exp, bias=cstb,
                                 scale=L1_SCALE, after=[tok, tcst, bfree(E, "pt1")])
                    setfree(E, f"ps{b}", texp)
                    tok_ctx = p.mm(ps[cb][:, 0:256], vbuf[:, m * 128:(m + 1) * 128],
                                   PTst[:, :], start=(m == 0), stop=(m == nch - 1),
                                   after=[texp, vtoks] +
                                         ([bfree(E, f"ps{cb}")] if m == 0 else []),
                                   tag=True)
                    tok_den = p.mm(psA[0:1, 0:256], E["ones128"][:, :], PTst[:, :],
                                   start=(m == 0), stop=(m == nch - 1),
                                   after=[texp] + ([bfree(E, "psA")] if m == 0 else []),
                                   tag=True)
                    setfree(E, "pt1", tok_ctx, tok_den)
                trc = p.recip(E["row"][0:1, 0:256], psA[0:1, 0:256],
                              after=[tok_den, bfree(E, "rowA")])
                trb = p.scopy(E["rowb"][0:1, 0:256], E["row"][0:1, 0:256],
                              after=[trc, bfree(E, "rowb")])
                setfree(E, "rowA", trb)
                tbc = p.mm(psA[:, 0:256], E["ones1"][:, :], E["rowb"][0:1, 0:256],
                           start=True, stop=True, after=[trb, trc], tag=True)
                setfree(E, "rowb", tbc)
                tb2 = p.scopy(bcast[:, :], psA[:, 0:256], after=[tbc, bfree(E, "bcast")])
                setfree(E, "psA", tb2)
                tcx = p.tt(ctxsb[:, :], ps[cb][:, 0:256], bcast[:, :], ALU.mult,
                           after=[tb2, tok_ctx, bfree(E, "ctxsb")])
                setfree(E, f"ps{cb}", tcx)
                setfree(E, "bcast", tcx)
                p.dma("sync", E["ctx_dram"].ap()[h * 128:(h + 1) * 128,
                                                 qg * 256:(qg + 1) * 256],
                      ctxsb[:, :], after=[tcx])
                tsp = p.ckpt("sync")
                setfree(E, "ctxsb", tsp)
                prev_tok = tsp


def outproj(p, E, ar, layer):
    ps = E["ps"]
    ar.reset(E["att_mark"])
    ctxld = ar.alloc(16 * TPC, BF16)
    oslab = [ar.alloc(16 * 128, BF16), ar.alloc(16 * 128, BF16)]
    dtmp = ar.alloc(512, F32)
    wname = "outw" if layer == 0 else "wo"

    for c in range(16):
        p.dma("sync", ctxld[:, c * TPC:(c + 1) * TPC],
              E["ctx_dram"].ap()[c * 128:(c + 1) * 128, :])
    tok_ld = [p.ckpt("sync")]
    for ft in range(16):
        iw = ft % 2
        p.dma("sync", oslab[iw][:, :], _w_view(E, wname, ft),
              after=[bfree(E, f"oslab{iw}"), E["tokW"]])
        wt = p.ckpt("sync")
        tok_last = None
        for g in range(2):
            b = g % 2
            tok = None
            for c in range(16):
                tok = p.mm(ps[b][:, :], oslab[iw][:, c * 128:(c + 1) * 128],
                           ctxld[:, c * TPC + g * 512: c * TPC + (g + 1) * 512],
                           start=(c == 0), stop=(c == 15),
                           after=([wt, bfree(E, f"ps{b}")] + tok_ld if c == 0 else []),
                           tag=(c == 15))
            tok_last = tok
            if layer == 0:
                td = p.act(dtmp[:, :], ps[b][:, :], ACT.Identity,
                           bias=E["outb"][:, ft:ft + 1], scale=1.0,
                           after=[tok, bfree(E, "dtmp")])
            else:
                td = p.scopy(dtmp[:, :], ps[b][:, :], after=[tok, bfree(E, "dtmp")])
            setfree(E, f"ps{b}", td)
            last_add = None
            for u in range(4):
                bb = 2 + u % 2
                ttr = p.transpose(ps[bb][:, 0:128].bitcast(F32),
                                  dtmp[:, u * 128:(u + 1) * 128], E["identf"][:, :],
                                  after=[td, bfree(E, f"ps{bb}")])
                tt_ = g * 4 + u
                xsl = E["x_res"][:, tt_ * D + ft * 128: tt_ * D + (ft + 1) * 128]
                last_add = p.tt(xsl, ps[bb][:, 0:128], xsl, ALU.add, after=[ttr])
                setfree(E, f"ps{bb}", last_add)
            setfree(E, "dtmp", last_add)
        setfree(E, f"oslab{iw}", tok_last)


def phase_C(p, E, ar, layer):
    if layer == 0:
        attn0(p, E, ar)
    else:
        attn1(p, E, ar)
    outproj(p, E, ar, layer)



def kernel(L0_attn_x, L0_attn_qkv_w, L0_attn_qkv_b, L0_attn_out_w, L0_attn_out_b,
           L0_ffn_w13, L0_ffn_w2, L0_ffn_norm_w, L1_attn_wq, L1_attn_wk, L1_attn_wv,
           L1_attn_wo, L1_ffn_A, L1_ffn_Bw, L1_ffn_Wg):
    inp = dict(L0_attn_x=L0_attn_x, L0_attn_qkv_w=L0_attn_qkv_w,
               L0_attn_qkv_b=L0_attn_qkv_b, L0_attn_out_w=L0_attn_out_w,
               L0_attn_out_b=L0_attn_out_b, L0_ffn_w13=L0_ffn_w13,
               L0_ffn_w2=L0_ffn_w2, L0_ffn_norm_w=L0_ffn_norm_w,
               L1_attn_wq=L1_attn_wq, L1_attn_wk=L1_attn_wk, L1_attn_wv=L1_attn_wv,
               L1_attn_wo=L1_attn_wo, L1_ffn_A=L1_ffn_A, L1_ffn_Bw=L1_ffn_Bw,
               L1_ffn_Wg=L1_ffn_Wg)
    went, wflat = _pack_weights(inp)
    wtot = wflat.size
    shard = wtot // NCORE
    C, S2 = _rope_tables()
    ramps = _alibi_ramps()
    x = np.asarray(L0_attn_x, dtype=np.float32).reshape(S, D)

    qkvb_t = np.ascontiguousarray(
        np.asarray(L0_attn_qkv_b, dtype=np.float32).reshape(48, 128).T)
    outb_t = np.ascontiguousarray(
        np.asarray(L0_attn_out_b, dtype=np.float32).reshape(16, 128).T)
    normw_t = np.ascontiguousarray(
        np.asarray(L0_ffn_norm_w, dtype=np.float32).reshape(44, 128).T)

    bf = ml_dtypes.bfloat16
    em = np.zeros((8, 512), dtype=bf)
    for e_ in range(8):
        em[e_, e_ * 64:(e_ + 1) * 64] = 1

    nc = build_nc(went, wtot)

    in_maps = []
    for c in range(NCORE):
        toks = np.arange(c, S, 8)
        m0, m1, mT = _masks(c)
        in_maps.append({
            "xin": np.ascontiguousarray(x[toks]),
            "wsh": np.ascontiguousarray(wflat[c * shard:(c + 1) * shard]),
            "ropeC": np.ascontiguousarray(C[:, toks]),
            "ropeS": np.ascontiguousarray(S2[:, toks]),
            "ramps": ramps,
            "qkvb": qkvb_t,
            "outb": outb_t,
            "normw": normw_t,
            "m0": m0, "m1": m1, "mT": mT,
            "identf": np.eye(128, dtype=np.float32),
            "identb": np.eye(128, dtype=bf),
            "ones2": np.ones((2, 128), dtype=bf),
            "ones1": np.ones((1, 128), dtype=bf),
            "ones128": np.ones((128, 1), dtype=bf),
            "emask": em,
        })
    res = run_bass_kernel_spmd(nc, in_maps, core_ids=list(range(NCORE)))
    global _LAST_RES
    _LAST_RES = res
    out = np.empty((S, D), dtype=np.float32)
    for c in range(NCORE):
        out[np.arange(c, S, 8)] = np.asarray(res.results[c]["yout"])
    return out.reshape(1, S, D)
